# version 01
# baseline (speedup 1.0000x reference)
import math
from contextlib import ExitStack

import numpy as np
import concourse.bass as bass
import concourse.mybir as mybir
from concourse.bass_utils import run_bass_kernel_spmd
from concourse.aot_env import allow_ici_all_to_all

F32 = mybir.dt.float32
BF16 = mybir.dt.bfloat16
AF = mybir.ActivationFunctionType
ALU = mybir.AluOpType
AX = mybir.AxisListType

D = 2048
T = 8192
NCORE = 8
TL = T // NCORE
NH = 16
HD = 128
HPC = NH // NCORE
NDC = D // 128
QB = 1024
NKT = T // 128
INTER0 = 5632
NIC0 = INTER0 // 128
INTER1 = 1408
NIC1 = INTER1 // 128
SHINTER = 2816
NICS = SHINTER // 128
NE = 16
BIG = 1.0e30

ENGS = ("sync", "tensor", "scalar", "vector", "gpsimd")


class Sem:
    def __init__(self, stack, nc, name):
        self.h = stack.enter_context(nc.semaphore(name))
        self.n = 0


class Emitter:

    def __init__(self, nc, block, bar):
        self.nc = nc
        self.block = block
        self.bar = bar
        self.q = {e: [] for e in ENGS}

    def wait(self, eng, sem, val=None):
        v = sem.n if val is None else val
        if v <= 0:
            return
        self.q[eng].append(("wait", (lambda e, s=sem.h, v=v: e.wait_ge(s, v)), None))

    def op(self, eng, fn, sem=None, inc=1):
        if sem is not None:
            sem.n += inc
            g = lambda e, f=fn, s=sem.h, k=inc: f(e).then_inc(s, k)
            self.q[eng].append(("op", g, (sem, sem.n)))
        else:
            self.q[eng].append(("op", fn, None))

    def dma(self, eng, out_ap, in_ap, sem, **kw):
        sem.n += 16
        g = lambda e, o=out_ap, i=in_ap, s=sem.h, kw=kw: \
            e.dma_start(out=o, in_=i, **kw).then_inc(s, 16)
        self.q[eng].append(("op", g, (sem, sem.n)))

    def flush(self):
        for e in ENGS:
            if self.q[e]:
                ops = [f for (_, f, _) in self.q[e]]
                self.q[e] = []
                getattr(self.block, e)(lambda eng, ops=ops: [f(eng) for f in ops])

    def barrier(self, dma_sems=()):
        bar = self.bar
        targets = {}
        for e in ENGS:
            entries = self.q[e]
            last_op = None
            for idx in range(len(entries) - 1, -1, -1):
                if entries[idx][0] == "op":
                    last_op = idx
                    break
            if last_op is None:
                continue
            kind, fn, info = entries[last_op]
            if info is None:
                bar.n += 1
                g = lambda eng, f=fn, s=bar.h: f(eng).then_inc(s, 1)
                entries[last_op] = ("op", g, (bar, bar.n))
                targets[e] = (bar, bar.n)
            else:
                targets[e] = info
        alls = list(targets.values()) + [(s, s.n) for s in dma_sems if s.n > 0]
        bysem = {}
        for (sem, cnt) in alls:
            bysem[id(sem)] = (sem, max(cnt, bysem.get(id(sem), (sem, 0))[1]))
        for e in ENGS:
            for (sem, cnt) in bysem.values():
                self.wait(e, sem, cnt)


def build_program(subs=(0, 1, 2, 3), debug=False):
    nc = bass.Bass("TRN2", target_bir_lowering=False)
    st = ExitStack()
    uid = [0]

    def uname(base):
        uid[0] += 1
        return f"{base}_{uid[0]}"

    def dram(name, shape, dtype=F32, kind=None):
        if kind is None:
            return nc.dram_tensor(name, shape, dtype).ap()
        return nc.dram_tensor(name, shape, dtype, kind=kind).ap()

    ext = {}

    def ein(name, shape, dtype=F32):
        ext[name] = dram(name, shape, dtype, kind="ExternalInput")
        return ext[name]

    need = set(subs)
    has_attn = (0 in need) or (2 in need)
    x_first = ein("x0" if subs[0] == 0 else "xin", [D, TL])
    xout = dram("xout", [D, TL], F32, kind="ExternalOutput")

    if 0 in need:
        ein("l0_wq", [D, HPC * HD]); ein("l0_wk", [D, HPC * HD]); ein("l0_wv", [D, HPC * HD])
        ein("l0_wo", [NDC, D, 128])
        ein("c_tri", [128, 8 * 1024])
    if 1 in need:
        ein("l0_gw", [NIC0, D, 128]); ein("l0_uw", [NIC0, D, 128])
        ein("l0_dw", [NDC, INTER0, 128])
        ein("l0_gs", [128, NIC0]); ein("l0_us", [128, NIC0]); ein("l0_ds", [128, NDC])
    if 2 in need:
        ein("l1_wq", [D, HPC * HD]); ein("l1_wk", [D, HPC * HD]); ein("l1_wv", [D, HPC * HD])
        ein("l1_wo", [NDC, D, 128])
        ein("l1_bq", [128, HPC]); ein("l1_bk", [128, HPC]); ein("l1_bv", [1, HPC * HD])
        ein("l1_bo", [128, NDC]); ein("l1_gates", [1, 8])
    if 3 in need:
        ein("l1_wg", [D, NE])
        ein("l1_w1", [NE, NIC1, D, 128]); ein("l1_w3", [NE, NIC1, D, 128])
        ein("l1_w2", [NE, NDC, INTER1, 128])
        ein("l1_ws1", [NICS, D, 128]); ein("l1_ws3", [NICS, D, 128])
        ein("l1_ws2", [NDC, SHINTER, 128])
        ein("c_onehot", [NE, NE * 128])
        ein("c_id", [128, 128])
    ein("c_ones", [128, 128])
    if has_attn:
        ein("c_pos", [128, NKT])

    xd = [dram("xd_a", [D, TL]), dram("xd_b", [D, TL])]
    yd = dram("yd", [D, TL])
    if has_attn:
        xnl = dram("xnl", [D, TL], BF16)
        xng = dram("xng", [NCORE * D, TL], BF16)
        zl = dram("zl", [NCORE * HPC * 128, QB], BF16)
        zato = dram("zato", [NCORE * HPC * 128, QB], BF16)
        qd = dram("qd", [HPC * 128, T], BF16)

    sW = Sem(st, nc, "sW")
    sX = Sem(st, nc, "sX")
    sS = Sem(st, nc, "sS")
    sT = Sem(st, nc, "sT")
    sA = Sem(st, nc, "sA")
    sV = Sem(st, nc, "sV")
    sB = Sem(st, nc, "sB")
    sC = [Sem(st, nc, f"sC{i}") for i in range(6)]
    n_cc = [0]

    block = st.enter_context(nc.Block())
    em = Emitter(nc, block, sB)

    def psbuf(base, shape, dtype):
        return st.enter_context(nc.sbuf_tensor(uname(base), shape, dtype))

    xs = psbuf("xs", [128, 2 * 1024], F32)
    ones32 = psbuf("ones32", [128, 128], F32)
    ones16 = psbuf("ones16", [128, 128], BF16)
    rows = psbuf("rows", [1, 4224], F32)
    ab = psbuf("ab", [128, 2 * 1024], F32)
    sqs = psbuf("sqs", [128, 1024], BF16)
    xbf = psbuf("xbf", [128, 1024], BF16)
    tsc = psbuf("tsc", [128, 1024], F32)
    smalls = psbuf("smalls", [128, 16], F32)
    ystage = psbuf("ystage", [128, 2 * 1024], F32)
    if has_attn:
        pos_sb = psbuf("pos_sb", [128, NKT], F32)
        biasr = psbuf("biasr", [128, 2 * NKT], F32)

    em.dma("gpsimd", ones32[:, :], ext["c_ones"], sW)
    em.dma("gpsimd", ones16[:, :], ext["c_ones"], sW)
    if has_attn:
        em.dma("gpsimd", pos_sb[:, :], ext["c_pos"], sW)
    em.op("vector", lambda e: e.memset(smalls[0:1, 10:11], 1e-5), sV)
    eps_ap = smalls[0:1, 10:11]
    em.wait("tensor", sW)
    em.wait("vector", sW)
    em.wait("scalar", sW)
    em.wait("scalar", sV)

    scope = {"stack": None}

    def phase_open():
        scope["stack"] = ExitStack()

    def phase_close():
        em.barrier(dma_sems=(sW, sX, sS))
        em.flush()
        scope["stack"].close()
        scope["stack"] = None

    def sb(base, shape, dtype):
        return scope["stack"].enter_context(nc.sbuf_tensor(uname(base), shape, dtype))

    def psum(base, shape, dtype=F32):
        return scope["stack"].enter_context(nc.psum_tensor(uname(base), shape, dtype))

    stats_mm = {"n": 0}

    def _stats_chunk(src_ap, d, ps_sum, ps_sq, act_ready):
        em.wait("scalar", sT, stats_mm["n"])
        em.op("scalar", lambda e, s=src_ap: e.activation(xbf[:, :], s, AF.Copy), sA)
        em.op("scalar", lambda e, s=src_ap: e.activation(sqs[:, :], s, AF.Square), sA)
        act_ready.append(sA.n)
        em.wait("tensor", sA)
        for half in range(2):
            sl = slice(half * 512, half * 512 + 512)
            last = (half == 1)
            em.op("tensor", lambda e, sl=sl, d=d: e.matmul(
                ps_sum[0:1, sl], ones16[:, 0:1], xbf[:, sl],
                start=(d == 0), stop=(d == NDC - 1)), None)
            em.op("tensor", lambda e, sl=sl, d=d: e.matmul(
                ps_sq[0:1, sl], ones16[:, 0:1], sqs[:, sl],
                start=(d == 0), stop=(d == NDC - 1)), sT if last else None)
        stats_mm["n"] = sT.n

    def ln_stats_dram(xsrc):
        ps_sum = psum("ps_sum", [1, 1024])
        ps_sq = psum("ps_sq", [1, 1024])
        act_ready = []
        dma_done = []
        for d in range(NDC):
            buf = xs[:, (d % 2) * 1024:(d % 2) * 1024 + 1024]
            if d >= 2:
                em.wait("sync", sA, act_ready[d - 2])
            em.dma("sync", buf, xsrc[d * 128:(d + 1) * 128, :], sX)
            dma_done.append(sX.n)
            em.wait("scalar", sX, dma_done[d])
            _stats_chunk(buf, d, ps_sum, ps_sq, act_ready)
        _ln_finish(ps_sum, ps_sq)

    def ln_stats_sbuf(src_fn):
        ps_sum = psum("ps_sum", [1, 1024])
        ps_sq = psum("ps_sq", [1, 1024])
        act_ready = []
        for d in range(NDC):
            _stats_chunk(src_fn(d), d, ps_sum, ps_sq, act_ready)
        _ln_finish(ps_sum, ps_sq)

    def _ln_finish(ps_sum, ps_sq):
        m = rows[0:1, 0:1024]; msq = rows[0:1, 1024:2048]
        var = rows[0:1, 2048:3072]; sd = rows[0:1, 3072:4096]
        r = rows[0:1, 2048:3072]; b = rows[0:1, 1024:2048]
        em.wait("vector", sT)
        em.op("vector", lambda e: e.tensor_scalar_mul(m, ps_sum[0:1, :], 1.0 / D))
        em.op("vector", lambda e: e.tensor_scalar_mul(msq, ps_sq[0:1, :], 1.0 / D))
        em.op("vector", lambda e: e.tensor_mul(var, m, m))
        em.op("vector", lambda e: e.tensor_sub(var, msq, var), sV)
        em.wait("scalar", sV)
        em.op("scalar", lambda e: e.activation(sd, var, AF.Sqrt, bias=eps_ap), sA)
        em.wait("vector", sA)
        em.op("vector", lambda e: e.reciprocal(r, sd))
        em.op("vector", lambda e: e.scalar_tensor_tensor(
            b, m, -1.0, r, ALU.mult, ALU.mult), sV)
        ps_bc = psum("ps_bc", [128, 1024])
        em.wait("tensor", sV)
        for half in range(2):
            sl = slice(half * 512, half * 512 + 512)
            em.op("tensor", lambda e, sl=sl: e.matmul(
                ps_bc[:, sl], ones32[0:1, :], rows[0:1, 2048 + sl.start:2048 + sl.stop],
                start=True, stop=True), sT if half == 1 else None)
        em.wait("vector", sT)
        em.op("vector", lambda e: e.tensor_copy(ab[:, 0:1024], ps_bc[:, :]), sV)
        em.wait("tensor", sV)
        for half in range(2):
            sl = slice(half * 512, half * 512 + 512)
            em.op("tensor", lambda e, sl=sl: e.matmul(
                ps_bc[:, sl], ones32[0:1, :], rows[0:1, 1024 + sl.start:1024 + sl.stop],
                start=True, stop=True), sT if half == 1 else None)
        em.wait("vector", sT)
        em.op("vector", lambda e: e.tensor_copy(ab[:, 1024:2048], ps_bc[:, :]), sV)

    def norm_chunk(xchunk_ap, out_ap, sem=None):
        em.op("vector", lambda e: e.tensor_mul(tsc[:, :], xchunk_ap, ab[:, 0:1024]))
        em.op("vector", lambda e: e.tensor_add(out_ap, tsc[:, :], ab[:, 1024:2048]),
              sem if sem is not None else sV)

    def post_ln_residual_dram(xsrc, xdst):
        phase_open()
        ln_stats_dram(yd)
        phase_close()
        phase_open()
        xst = sb("xst", [128, 2 * 1024], F32)
        norm_done = []
        store_done = []
        for d in range(NDC):
            buf = xs[:, (d % 2) * 1024:(d % 2) * 1024 + 1024]
            ybf = ystage[:, (d % 2) * 1024:(d % 2) * 1024 + 1024]
            ost = xst[:, (d % 2) * 1024:(d % 2) * 1024 + 1024]
            if d >= 2:
                em.wait("sync", sV, norm_done[d - 2])
                em.wait("vector", sS, store_done[d - 2])
            em.dma("sync", ybf, yd[d * 128:(d + 1) * 128, :], sX)
            em.dma("sync", buf, xsrc[d * 128:(d + 1) * 128, :], sX)
            xn_d = sX.n
            em.wait("vector", sX, xn_d)
            em.op("vector", lambda e, ybf=ybf: e.tensor_mul(tsc[:, :], ybf, ab[:, 0:1024]))
            em.op("vector", lambda e: e.tensor_add(tsc[:, :], tsc[:, :], ab[:, 1024:2048]))
            em.op("vector", lambda e, ost=ost, buf=buf: e.tensor_add(ost, tsc[:, :], buf), sV)
            norm_done.append(sV.n)
            em.wait("scalar", sV, norm_done[d])
            em.dma("scalar", xdst[d * 128:(d + 1) * 128, :], ost, sS)
            store_done.append(sS.n)
        phase_close()

    def attention(layer, xsrc, xdst):
        pre = "l0_" if layer == 0 else "l1_"
        causal = (layer == 0)
        has_bias = (layer == 1)

        phase_open()
        ln_stats_dram(xsrc)
        phase_close()

        phase_open()
        xnstage = sb("xnst", [128, 2 * 1024], BF16)
        norm_done = []
        store_done = []
        for d in range(NDC):
            buf = xs[:, (d % 2) * 1024:(d % 2) * 1024 + 1024]
            ost = xnstage[:, (d % 2) * 1024:(d % 2) * 1024 + 1024]
            if d >= 2:
                em.wait("sync", sV, norm_done[d - 2])
                em.wait("vector", sS, store_done[d - 2])
            em.dma("sync", buf, xsrc[d * 128:(d + 1) * 128, :], sX)
            em.wait("vector", sX)
            norm_chunk(buf, ost)
            norm_done.append(sV.n)
            em.wait("scalar", sV, norm_done[d])
            em.dma("scalar", xnl[d * 128:(d + 1) * 128, :], ost, sS)
            store_done.append(sS.n)
        em.wait("gpsimd", sS)
        cc = sC[n_cc[0]]; n_cc[0] += 1
        em.op("gpsimd", lambda e: e.collective_compute(
            "AllGather", ALU.bypass, replica_groups=[list(range(NCORE))],
            ins=[xnl[:, :].opt()], outs=[xng[:, :].opt()]), cc)
        phase_close()

        sub_stack = ExitStack()
        kf = sub_stack.enter_context(nc.sbuf_tensor(uname("kf"), [128, HPC * T], BF16))
        vt = sub_stack.enter_context(nc.sbuf_tensor(uname("vt"), [128, NKT * HPC * 128], BF16))

        phase_open()
        wq = sb("wq", [128, NDC * HPC * HD], BF16)
        wk = sb("wk", [128, NDC * HPC * HD], BF16)
        wv = sb("wv", [128, NDC * HPC * HD], BF16)
        em.dma("gpsimd", wq.ap().rearrange("p (c f) -> p c f", c=NDC),
               ext[pre + "wq"].rearrange("(c p) f -> p c f", p=128), sW)
        em.dma("gpsimd", wk.ap().rearrange("p (c f) -> p c f", c=NDC),
               ext[pre + "wk"].rearrange("(c p) f -> p c f", p=128), sW)
        em.dma("gpsimd", wv.ap().rearrange("p (c f) -> p c f", c=NDC),
               ext[pre + "wv"].rearrange("(c p) f -> p c f", p=128), sW)
        ps_small = psum("ps_small", [128, 1024])
        if has_bias:
            bqk = sb("bqk", [128, 2 * HPC], F32)
            bvrow = sb("bvrow", [1, HPC * HD], BF16)
            gat = sb("gat", [1, 8], F32)
            em.dma("gpsimd", bqk[:, 0:HPC], ext[pre + "bq"], sW)
            em.dma("gpsimd", bqk[:, HPC:2 * HPC], ext[pre + "bk"], sW)
            em.dma("gpsimd", bvrow[0:1, :], ext[pre + "bv"], sW)
            em.dma("gpsimd", gat[0:1, :], ext[pre + "gates"], sW)
            em.wait("scalar", sW)
            em.op("scalar", lambda e: e.activation(rows[0:1, 4096:4104], gat[0:1, :],
                                                   AF.Sigmoid), sA)
            em.wait("vector", sA)
            em.op("vector", lambda e: e.tensor_scalar(
                rows[0:1, 4104:4105], rows[0:1, 4099:4100],
                -0.1 / math.sqrt(HD), 1.0 / math.sqrt(HD), ALU.mult, ALU.add))
            em.op("vector", lambda e: e.tensor_scalar_mul(
                rows[0:1, 4105:4106], rows[0:1, 4098:4099], 0.05), sV)
            em.wait("tensor", sV)
            em.op("tensor", lambda e: e.matmul(
                ps_small[:, 0:2], ones32[0:1, :], rows[0:1, 4104:4106],
                start=True, stop=True), sT)
            em.wait("vector", sT)
            em.op("vector", lambda e: e.tensor_copy(smalls[:, 0:2], ps_small[:, 0:2]), sV)
            em.wait("vector", sV)
            em.op("vector", lambda e: e.tensor_scalar_mul(
                biasr[:, 0:NKT], pos_sb[:, :], smalls[:, 1:2]), sV)
        xg = sb("xg", [128, NDC * 1024], BF16)
        qstage = sb("qstage", [128, HPC * 1024], BF16)
        psA_ = psum("psA_", [128, 1024])
        psB_ = psum("psB_", [128, 1024])
        em.wait("sync", cc, 1)
        em.wait("tensor", sW)
        qst_store = 0
        norm_read = 0
        for tb in range(8):
            em.wait("sync", sT)
            em.dma("sync", xg.ap().rearrange("p (c t) -> p c t", c=NDC),
                   xng[tb * D:(tb + 1) * D, :].rearrange("(c p) t -> p c t", p=128), sX)
            xg_d = sX.n
            em.wait("tensor", sX, xg_d)
            em.wait("tensor", sA)
            em.wait("tensor", sV)
            pq = [psA_, psB_]
            for hh in range(HPC):
                for dc in range(NDC):
                    for half in range(2):
                        sl = slice(half * 512, half * 512 + 512)
                        em.op("tensor", lambda e, hh=hh, dc=dc, sl=sl: e.matmul(
                            pq[hh][:, sl],
                            wq[:, dc * HPC * HD + hh * HD: dc * HPC * HD + (hh + 1) * HD],
                            xg[:, dc * 1024 + sl.start: dc * 1024 + sl.stop],
                            start=(dc == 0), stop=(dc == NDC - 1)),
                            sT if (dc == NDC - 1 and half == 1) else None)
            q_mm = sT.n
            if has_bias:
                em.wait("vector", sT, q_mm)
                em.wait("vector", sX, qst_store)
                for hh in range(HPC):
                    qap = qstage[:, hh * 1024:(hh + 1) * 1024]
                    em.op("vector", lambda e, hh=hh, qap=qap: e.tensor_scalar(
                        qap, pq[hh][:, :], bqk[:, hh:hh + 1], smalls[:, 0:1],
                        ALU.add, ALU.mult), sV)
                q_ready_sem, q_ready = sV, sV.n
            else:
                em.wait("scalar", sT, q_mm)
                em.wait("scalar", sX, qst_store)
                for hh in range(HPC):
                    qap = qstage[:, hh * 1024:(hh + 1) * 1024]
                    em.op("scalar", lambda e, hh=hh, qap=qap: e.activation(
                        qap, pq[hh][:, :], AF.Copy, scale=1.0 / math.sqrt(HD)), sA)
                q_ready_sem, q_ready = sA, sA.n
            em.wait("sync", q_ready_sem, q_ready)
            for hh in range(HPC):
                em.dma("sync", qd[hh * 128:(hh + 1) * 128, tb * 1024:(tb + 1) * 1024],
                       qstage[:, hh * 1024:(hh + 1) * 1024], sX)
            qst_store = sX.n
            em.wait("tensor", q_ready_sem, q_ready)
            for hh in range(HPC):
                for dc in range(NDC):
                    for half in range(2):
                        sl = slice(half * 512, half * 512 + 512)
                        em.op("tensor", lambda e, hh=hh, dc=dc, sl=sl: e.matmul(
                            pq[hh][:, sl],
                            wk[:, dc * HPC * HD + hh * HD: dc * HPC * HD + (hh + 1) * HD],
                            xg[:, dc * 1024 + sl.start: dc * 1024 + sl.stop],
                            start=(dc == 0), stop=(dc == NDC - 1)),
                            sT if (dc == NDC - 1 and half == 1) else None)
            k_mm = sT.n
            if has_bias:
                em.wait("vector", sT, k_mm)
                for hh in range(HPC):
                    kap = kf[:, hh * T + tb * 1024: hh * T + (tb + 1) * 1024]
                    em.op("vector", lambda e, hh=hh, kap=kap: e.tensor_scalar_add(
                        kap, pq[hh][:, :], bqk[:, HPC + hh:HPC + hh + 1]), sV)
                ev_k_sem, ev_k = sV, sV.n
            else:
                em.wait("scalar", sT, k_mm)
                for hh in range(HPC):
                    kap = kf[:, hh * T + tb * 1024: hh * T + (tb + 1) * 1024]
                    em.op("scalar", lambda e, hh=hh, kap=kap: e.activation(
                        kap, pq[hh][:, :], AF.Copy), sA)
                ev_k_sem, ev_k = sA, sA.n
            for hh in range(HPC):
                for which in range(2):
                    src = (qstage[:, hh * 1024:(hh + 1) * 1024] if which == 0
                           else kf[:, hh * T + tb * 1024: hh * T + (tb + 1) * 1024])
                    if which == 0:
                        em.wait("scalar", q_ready_sem, q_ready)
                    else:
                        em.wait("scalar", ev_k_sem, ev_k)
                    em.wait("scalar", sT, stats_mm["n"])
                    em.op("scalar", lambda e, src=src: e.activation(
                        sqs[:, :], src, AF.Square), sA)
                    em.wait("tensor", sA)
                    em.wait("tensor", sV, norm_read)
                    for half in range(2):
                        sl = slice(half * 512, half * 512 + 512)
                        em.op("tensor", lambda e, sl=sl: e.matmul(
                            ps_small[0:1, sl], ones16[:, 0:1], sqs[:, sl],
                            start=True, stop=True), sT if half == 1 else None)
                    stats_mm["n"] = sT.n
                    em.wait("vector", sT)
                    col = 4 + 2 * which + hh
                    em.op("vector", lambda e: e.reduce_max(
                        rows[0:1, 4106:4107], ps_small[0:1, :], AX.X), sV)
                    em.wait("vector", sV)
                    if tb == 0:
                        em.op("vector", lambda e, col=col: e.tensor_copy(
                            smalls[0:1, col:col + 1], rows[0:1, 4106:4107]), sV)
                    else:
                        em.op("vector", lambda e, col=col: e.tensor_max(
                            smalls[0:1, col:col + 1], smalls[0:1, col:col + 1],
                            rows[0:1, 4106:4107]), sV)
                    norm_read = sV.n
            em.wait("tensor", ev_k_sem, ev_k)
            ev_v = [0, 0]
            for tt in range(8):
                pv = pq[tt % 2]
                col0 = (tt // 2) * 256
                em.wait("tensor", sA, ev_v[tt % 2])
                for dc in range(NDC):
                    em.op("tensor", lambda e, tt=tt, dc=dc, pv=pv, col0=col0: e.matmul(
                        pv[:, col0:col0 + 256],
                        xg[:, dc * 1024 + tt * 128: dc * 1024 + tt * 128 + 128],
                        wv[:, dc * HPC * HD:(dc + 1) * HPC * HD],
                        start=(dc == 0), stop=(dc == NDC - 1) and not has_bias),
                        sT if (dc == NDC - 1 and not has_bias) else None)
                if has_bias:
                    em.op("tensor", lambda e, pv=pv, col0=col0: e.matmul(
                        pv[:, col0:col0 + 256], ones16[0:1, :], bvrow[0:1, :],
                        start=False, stop=True), sT)
                vmm = sT.n
                em.wait("scalar", sT, vmm)
                em.op("scalar", lambda e, tt=tt, pv=pv, col0=col0, tb=tb: e.activation(
                    vt[:, (tb * 8 + tt) * 256:(tb * 8 + tt) * 256 + 256],
                    pv[:, col0:col0 + 256], AF.Copy), sA)
                ev_v[tt % 2] = sA.n
        em.wait("scalar", sV)
        for hh in range(HPC):
            em.op("scalar", lambda e, hh=hh: e.activation(
                rows[0:1, 4108 + hh:4109 + hh], smalls[0:1, 4 + hh:5 + hh], AF.Sqrt), sA)
            em.op("scalar", lambda e, hh=hh: e.activation(
                rows[0:1, 4110 + hh:4111 + hh], smalls[0:1, 6 + hh:7 + hh], AF.Sqrt), sA)
        em.wait("vector", sA)
        for hh in range(HPC):
            if causal:
                em.op("vector", lambda e, hh=hh: e.tensor_mul(
                    rows[0:1, 4112 + hh:4113 + hh], rows[0:1, 4108 + hh:4109 + hh],
                    rows[0:1, 4110 + hh:4111 + hh]), sV)
            else:
                em.op("vector", lambda e, hh=hh: e.scalar_tensor_tensor(
                    rows[0:1, 4112 + hh:4113 + hh], rows[0:1, 4108 + hh:4109 + hh],
                    1.0, rows[0:1, 4110 + hh:4111 + hh], ALU.mult, ALU.mult), sV)
                em.wait("vector", sV)
                em.op("vector", lambda e, hh=hh: e.tensor_scalar_add(
                    rows[0:1, 4112 + hh:4113 + hh], rows[0:1, 4112 + hh:4113 + hh],
                    0.05), sV)
        em.wait("tensor", sV)
        em.wait("tensor", sV, norm_read)
        em.op("tensor", lambda e: e.matmul(ps_small[:, 0:2], ones32[0:1, :],
                                           rows[0:1, 4112:4114], start=True, stop=True), sT)
        em.wait("vector", sT)
        em.op("vector", lambda e: e.tensor_scalar_mul(smalls[:, 8:10], ps_small[:, 0:2], -1.0), sV)
        em.wait("vector", sV)
        if not causal:
            em.op("vector", lambda e: e.tensor_copy(tsc[:, 0:NKT], biasr[:, 0:NKT]), sV)
            em.wait("vector", sV)
            em.op("vector", lambda e: e.tensor_scalar_add(
                biasr[:, 0:NKT], tsc[:, 0:NKT], smalls[:, 8:9]))
            em.op("vector", lambda e: e.tensor_scalar_add(
                biasr[:, NKT:2 * NKT], tsc[:, 0:NKT], smalls[:, 9:10]), sV)
        if debug:
            dsm = dram(f"dbg_smalls{layer}", [128, 16], F32, kind="ExternalOutput")
            drw = dram(f"dbg_rows{layer}", [1, 4224], F32, kind="ExternalOutput")
            em.barrier(dma_sems=(sW, sX, sS))
            em.dma("gpsimd", dsm, smalls[:, :], sW)
            em.dma("gpsimd", drw, rows[0:1, :], sW)
        phase_close()

        phase_open()
        psS = [psum("psS0", [128, 1024]), psum("psS1", [128, 1024])]
        psO = psum("psO", [128, 1024])
        pt = sb("pt", [128, 3 * 1024], BF16)
        lacc = sb("lacc", [128, 1024], F32)
        rlb = sb("rlb", [128, 1024], F32)
        zst = sb("zst", [128, 2 * 1024], BF16)
        qst2 = sb("qst2", [128, 2 * 1024], BF16)
        flrows = sb("flrows", [1, 2048], F32)
        lrow = flrows[0:1, 0:1024]
        rrow = flrows[0:1, 1024:2048]
        if causal:
            tri_sb = sb("tri_sb", [128, 8 * 1024], F32)
            em.dma("gpsimd", tri_sb[:, :], ext["c_tri"], sW)
            em.wait("vector", sW)
        sS_done = {0: 0, 1: 0}
        sS_vdone = {0: 0, 1: 0}
        pt_need = [(0, 0)] * 3
        psO_free = 0
        q_used = {0: 0, 1: 0}
        z_used = {0: 0, 1: 0}
        for hh in range(HPC):
            for qb in range(8):
                nkt = 8 * (qb + 1) if causal else NKT
                qslot = qb % 2
                em.wait("sync", sT, q_used[qslot])
                em.dma("sync", qst2[:, qslot * 1024:(qslot + 1) * 1024],
                       qd[hh * 128:(hh + 1) * 128, qb * 1024:(qb + 1) * 1024], sX)
                q_d = sX.n
                qap = qst2[:, qslot * 1024:(qslot + 1) * 1024]
                for kt in range(nkt):
                    si = kt % 2
                    pti = kt % 3
                    em.wait("tensor", sA, sS_done[si])
                    em.wait("tensor", sV, sS_vdone[si])
                    if kt == 0:
                        em.wait("tensor", sX, q_d)
                        em.wait("tensor", sV, psO_free)
                    for half in range(2):
                        sl = slice(half * 512, half * 512 + 512)
                        em.op("tensor", lambda e, hh=hh, kt=kt, sl=sl, si=si, qap=qap: e.matmul(
                            psS[si][:, sl],
                            kf[:, hh * T + kt * 128: hh * T + kt * 128 + 128],
                            qap[:, sl], start=True, stop=True),
                            sT if half == 1 else None)
                    smm = sT.n
                    diag = causal and kt >= nkt - 8
                    if diag:
                        j = kt - (nkt - 8)
                        em.wait("vector", sT, smm)
                        em.op("vector", lambda e, j=j, si=si: e.tensor_add(
                            psS[si][:, :], psS[si][:, :],
                            tri_sb[:, j * 1024:(j + 1) * 1024]), sV)
                        sS_vdone[si] = sV.n
                        em.wait("scalar", sV, sS_vdone[si])
                    else:
                        em.wait("scalar", sT, smm)
                    need_pv, need_l = pt_need[pti]
                    em.wait("scalar", sT, need_pv)
                    em.wait("scalar", sV, need_l)
                    bias_ap = (smalls[:, 8 + hh:9 + hh] if causal
                               else biasr[:, hh * NKT + kt: hh * NKT + kt + 1])
                    em.op("scalar", lambda e, si=si, pti=pti, b=bias_ap: e.activation(
                        pt[:, pti * 1024:(pti + 1) * 1024], psS[si][:, :], AF.Exp,
                        bias=b), sA)
                    sS_done[si] = sA.n
                    exp_n = sA.n
                    em.wait("vector", sA, exp_n)
                    if kt == 0:
                        em.op("vector", lambda e, pti=pti: e.tensor_copy(
                            lacc[:, :], pt[:, pti * 1024:(pti + 1) * 1024]), sV)
                    else:
                        em.op("vector", lambda e, pti=pti: e.tensor_add(
                            lacc[:, :], lacc[:, :], pt[:, pti * 1024:(pti + 1) * 1024]), sV)
                    l_n = sV.n
                    em.wait("tensor", sA, exp_n)
                    for half in range(2):
                        sl = slice(half * 512, half * 512 + 512)
                        em.op("tensor", lambda e, hh=hh, kt=kt, sl=sl, pti=pti, nkt=nkt: e.matmul(
                            psO[:, sl],
                            vt[:, kt * 256 + hh * 128: kt * 256 + hh * 128 + 128],
                            pt[:, pti * 1024 + sl.start: pti * 1024 + sl.stop],
                            start=(kt == 0), stop=(kt == nkt - 1)),
                            sT if half == 1 else None)
                    pt_need[pti] = (sT.n, l_n)
                q_used[qslot] = sT.n
                em.wait("tensor", sV)
                em.wait("tensor", sA, sS_done[0])
                em.wait("tensor", sV, sS_vdone[0])
                for half in range(2):
                    sl = slice(half * 512, half * 512 + 512)
                    em.op("tensor", lambda e, sl=sl: e.matmul(
                        psS[0][0:1, sl], ones32[:, 0:1], lacc[:, sl],
                        start=True, stop=True), sT if half == 1 else None)
                em.wait("vector", sT)
                em.op("vector", lambda e: e.tensor_copy(lrow, psS[0][0:1, 0:1024]), sV)
                em.wait("vector", sV)
                em.op("vector", lambda e: e.reciprocal(rrow, lrow), sV)
                sS_vdone[0] = sV.n
                em.wait("tensor", sV)
                em.wait("tensor", sA, sS_done[1])
                em.wait("tensor", sV, sS_vdone[1])
                for half in range(2):
                    sl = slice(half * 512, half * 512 + 512)
                    em.op("tensor", lambda e, sl=sl: e.matmul(
                        psS[1][:, sl], ones32[0:1, :], rrow[0:1, sl],
                        start=True, stop=True), sT if half == 1 else None)
                em.wait("vector", sT)
                em.op("vector", lambda e: e.tensor_copy(rlb[:, :], psS[1][:, :]), sV)
                sS_vdone[1] = sV.n
                zslot = (hh * 8 + qb) % 2
                em.wait("vector", sS, z_used[zslot])
                em.op("vector", lambda e, zslot=zslot: e.tensor_mul(
                    zst[:, zslot * 1024:(zslot + 1) * 1024], psO[:, :], rlb[:, :]), sV)
                psO_free = sV.n
                em.wait("scalar", sV, psO_free)
                em.dma("scalar", zl[(qb * HPC + hh) * 128:(qb * HPC + hh + 1) * 128, :],
                       zst[:, zslot * 1024:(zslot + 1) * 1024], sS)
                z_used[zslot] = sS.n
        em.wait("gpsimd", sS)
        cc2 = sC[n_cc[0]]; n_cc[0] += 1

        def _a2a(e):
            with allow_ici_all_to_all():
                return e.collective_compute(
                    "AllToAll", ALU.bypass, replica_groups=[list(range(NCORE))],
                    ins=[zl[:, :].opt()], outs=[zato[:, :].opt()])
        em.op("gpsimd", _a2a, cc2)
        phase_close()
        sub_stack.close()

        phase_open()
        zsb = sb("zsb", [128, NDC * 1024], BF16)
        wo = sb("wo", [128, 2 * D], BF16)
        bo_sb = sb("bo_sb", [128, NDC], F32)
        if has_bias:
            em.dma("gpsimd", bo_sb[:, :], ext[pre + "bo"], sW)
        em.wait("sync", cc2, 1)
        em.dma("sync", zsb.ap().rearrange("p (c t) -> p c t", c=NDC),
               zato[:, :].rearrange("(c p) t -> p c t", p=128), sX)
        z_d = sX.n
        psY = [psum("psY0", [128, 1024]), psum("psY1", [128, 1024])]
        ev_done = [0, 0]
        w_used = [0, 0]
        yst_free = [0, 0]
        for oc in range(NDC):
            slot = oc % 2
            em.wait("gpsimd", sT, w_used[slot])
            em.dma("gpsimd", wo[:, slot * D:(slot + 1) * D].rearrange("p (c f) -> p c f", c=NDC),
                   ext[pre + "wo"][oc].rearrange("(c p) f -> p c f", p=128), sW)
            w_d = sW.n
            em.wait("tensor", sW, w_d)
            em.wait("tensor", sX, z_d)
            em.wait("tensor", sV if has_bias else sA, ev_done[slot])
            for zc in range(NDC):
                for half in range(2):
                    sl = slice(half * 512, half * 512 + 512)
                    em.op("tensor", lambda e, zc=zc, sl=sl, slot=slot: e.matmul(
                        psY[slot][:, sl],
                        wo[:, slot * D + zc * 128: slot * D + zc * 128 + 128],
                        zsb[:, zc * 1024 + sl.start: zc * 1024 + sl.stop],
                        start=(zc == 0), stop=(zc == NDC - 1)),
                        sT if (zc == NDC - 1 and half == 1) else None)
            w_used[slot] = sT.n
            em.wait("scalar", sT, sT.n)
            em.wait("scalar", sS, yst_free[slot])
            yap = ystage[:, slot * 1024:(slot + 1) * 1024]
            if has_bias:
                em.wait("vector", sT, sT.n)
                em.wait("vector", sS, yst_free[slot])
                em.op("vector", lambda e, slot=slot, yap=yap, oc=oc: e.tensor_scalar_add(
                    yap, psY[slot][:, :], bo_sb[:, oc:oc + 1]), sV)
                ev_done[slot] = sV.n
                ev_sem = sV
            else:
                em.op("scalar", lambda e, slot=slot, yap=yap: e.activation(
                    yap, psY[slot][:, :], AF.Copy), sA)
                ev_done[slot] = sA.n
                ev_sem = sA
            em.wait("scalar", ev_sem, ev_done[slot])
            em.dma("scalar", yd[oc * 128:(oc + 1) * 128, :], yap, sS)
            yst_free[slot] = sS.n
        phase_close()
        post_ln_residual_dram(xsrc, xdst)

    def ffn0(xsrc, xdst):
        phase_open()
        ln_stats_dram(xsrc)
        phase_close()
        phase_open()
        xn = sb("xn", [128, NDC * 1024], BF16)
        norm_done = []
        for d in range(NDC):
            buf = xs[:, (d % 2) * 1024:(d % 2) * 1024 + 1024]
            if d >= 2:
                em.wait("sync", sV, norm_done[d - 2])
            em.dma("sync", buf, xsrc[d * 128:(d + 1) * 128, :], sX)
            em.wait("vector", sX)
            norm_chunk(buf, xn[:, d * 1024:(d + 1) * 1024])
            norm_done.append(sV.n)
        gs_sb = sb("gs_sb", [128, NIC0], F32)
        us_sb = sb("us_sb", [128, NIC0], F32)
        ds_sb = sb("ds_sb", [128, NDC], F32)
        em.dma("gpsimd", gs_sb[:, :], ext["l0_gs"], sW)
        em.dma("gpsimd", us_sb[:, :], ext["l0_us"], sW)
        em.dma("gpsimd", ds_sb[:, :], ext["l0_ds"], sW)
        hbuf = sb("hbuf", [128, NIC0 * 1024], BF16)
        silu_sc = tsc
        wg = sb("wg", [128, 4 * D], BF16)
        wd = sb("wd", [128, INTER0], BF16)
        psG = psum("psG", [128, 1024])
        psU = psum("psU", [128, 1024])
        psD = [psum("psD0", [128, 1024]), psum("psD1", [128, 1024])]
        em.wait("scalar", sW)
        em.wait("tensor", sV)
        w_used = [0, 0]
        dw_used = [0, 0]
        dev_done = [0, 0]
        last_h = [0]
        yst_free = [0, 0]
        for ic in range(NIC0):
            slot = ic % 2
            em.wait("gpsimd", sT, w_used[slot])
            em.dma("gpsimd",
                   wg[:, (2 * slot) * D:(2 * slot + 1) * D].rearrange("p (c f) -> p c f", c=NDC),
                   ext["l0_gw"][ic].rearrange("(c p) f -> p c f", p=128), sW)
            em.dma("gpsimd",
                   wg[:, (2 * slot + 1) * D:(2 * slot + 2) * D].rearrange("p (c f) -> p c f", c=NDC),
                   ext["l0_uw"][ic].rearrange("(c p) f -> p c f", p=128), sW)
            em.wait("tensor", sW)
            em.wait("tensor", sA)
            em.wait("tensor", sV)
            for dc in range(NDC):
                for half in range(2):
                    sl = slice(half * 512, half * 512 + 512)
                    em.op("tensor", lambda e, dc=dc, slot=slot, sl=sl: e.matmul(
                        psG[:, sl],
                        wg[:, (2 * slot) * D + dc * 128:(2 * slot) * D + dc * 128 + 128],
                        xn[:, dc * 1024 + sl.start: dc * 1024 + sl.stop],
                        start=(dc == 0), stop=(dc == NDC - 1)), None)
            for dc in range(NDC):
                for half in range(2):
                    sl = slice(half * 512, half * 512 + 512)
                    em.op("tensor", lambda e, dc=dc, slot=slot, sl=sl: e.matmul(
                        psU[:, sl],
                        wg[:, (2 * slot + 1) * D + dc * 128:(2 * slot + 1) * D + dc * 128 + 128],
                        xn[:, dc * 1024 + sl.start: dc * 1024 + sl.stop],
                        start=(dc == 0), stop=(dc == NDC - 1)),
                        sT if (dc == NDC - 1 and half == 1) else None)
            w_used[slot] = sT.n
            em.wait("scalar", sT, sT.n)
            if ic > 0:
                em.wait("scalar", sV, last_h[0])
            em.op("scalar", lambda e, ic=ic: e.activation(
                silu_sc[:, :], psG[:, :], AF.Sigmoid, scale=gs_sb[:, ic:ic + 1]), sA)
            em.wait("vector", sA, sA.n)
            em.op("vector", lambda e, ic=ic: e.scalar_tensor_tensor(
                ystage[:, 0:1024], psG[:, :], gs_sb[:, ic:ic + 1],
                silu_sc[:, :], ALU.mult, ALU.mult))
            em.op("vector", lambda e, ic=ic: e.scalar_tensor_tensor(
                hbuf[:, ic * 1024:(ic + 1) * 1024], psU[:, :], us_sb[:, ic:ic + 1],
                ystage[:, 0:1024], ALU.mult, ALU.mult), sV)
            last_h[0] = sV.n
        h_all = sV.n
        for oc in range(NDC):
            slot = oc % 2
            em.wait("gpsimd", sT, dw_used[0])
            em.dma("gpsimd",
                   wd[:, :].rearrange("p (c f) -> p c f", c=NIC0),
                   ext["l0_dw"][oc].rearrange("(c p) f -> p c f", p=128), sW)
            em.wait("tensor", sW)
            em.wait("tensor", sA, dev_done[slot])
            em.wait("tensor", sV, h_all)
            for ic in range(NIC0):
                for half in range(2):
                    sl = slice(half * 512, half * 512 + 512)
                    em.op("tensor", lambda e, ic=ic, slot=slot, sl=sl: e.matmul(
                        psD[slot][:, sl],
                        wd[:, ic * 128: ic * 128 + 128],
                        hbuf[:, ic * 1024 + sl.start: ic * 1024 + sl.stop],
                        start=(ic == 0), stop=(ic == NIC0 - 1)),
                        sT if (ic == NIC0 - 1 and half == 1) else None)
            dw_used[0] = sT.n
            em.wait("scalar", sT, sT.n)
            em.wait("scalar", sS, yst_free[slot])
            yap = ystage[:, slot * 1024:(slot + 1) * 1024]
            em.op("scalar", lambda e, oc=oc, slot=slot, yap=yap: e.activation(
                yap, psD[slot][:, :], AF.Copy, scale=ds_sb[:, oc:oc + 1]), sA)
            dev_done[slot] = sA.n
            em.wait("scalar", sA, dev_done[slot])
            em.dma("scalar", yd[oc * 128:(oc + 1) * 128, :], yap, sS)
            yst_free[slot] = sS.n
        phase_close()
        post_ln_residual_dram(xsrc, xdst)

    def moe(xsrc, xdst):
        phase_open()
        ln_stats_dram(xsrc)
        phase_close()
        phase_open()
        ybuf = sb("ybuf", [128, NDC * 1024], F32)
        xn = sb("xn", [128, NDC * 1024], BF16)
        norm_done = []
        for d in range(NDC):
            buf = xs[:, (d % 2) * 1024:(d % 2) * 1024 + 1024]
            if d >= 2:
                em.wait("sync", sV, norm_done[d - 2])
            em.dma("sync", buf, xsrc[d * 128:(d + 1) * 128, :], sX)
            em.wait("vector", sX)
            norm_chunk(buf, xn[:, d * 1024:(d + 1) * 1024])
            norm_done.append(sV.n)
        oh_sb = sb("oh_sb", [16, NE * 128], F32)
        ct_sb = sb("ct_sb", [16, 1024], F32)
        if debug:
            ms_sb_dbg = sb("msdbg", [128, 16], F32)
            gmask_dbg = sb("gmdbg", [128, 16], F32)
            scdbg = sb("scdbg", [128, 8 * 16], F32)
        rt_stack = ExitStack()

        def rsb(base, shape, dtype):
            return rt_stack.enter_context(nc.sbuf_tensor(uname(base), shape, dtype))
        wgr = rsb("wgr", [128, NDC * NE], BF16)
        em.dma("gpsimd", wgr.ap().rearrange("p (c e) -> p c e", c=NDC),
               ext["l1_wg"].rearrange("(c p) e -> p c e", p=128), sW)
        id_sb = rsb("id_sb", [128, 128], F32)
        em.dma("gpsimd", id_sb[:, :], ext["c_id"], sW)
        em.dma("gpsimd", oh_sb[:, :], ext["c_onehot"], sW)
        rsc = rsb("rsc", [128, 64], F32)
        scores_sb = rsb("scores_sb", [128, 8 * 16], F32)
        gmask_sb = rsb("gmask_sb", [128, 16], F32)
        ms_sb = rsb("ms_sb", [128, 16], F32)
        col1 = rsb("col1", [128, 8], F32)
        psR = rt_stack.enter_context(nc.psum_tensor(uname("psR"), [128, 512], F32))
        psCT = rt_stack.enter_context(nc.psum_tensor(uname("psCT"), [16, 1024], F32))
        em.wait("tensor", sW)
        em.wait("tensor", sV)
        for tt in range(8):
            em.wait("tensor", sV)
            em.wait("tensor", sA)
            for dc in range(NDC):
                em.op("tensor", lambda e, tt=tt, dc=dc: e.matmul(
                    psR[:, 0:NE],
                    xn[:, dc * 1024 + tt * 128: dc * 1024 + tt * 128 + 128],
                    wgr[:, dc * NE:(dc + 1) * NE],
                    start=(dc == 0), stop=(dc == NDC - 1)),
                    sT if dc == NDC - 1 else None)
            em.wait("vector", sT)
            sc = scores_sb[:, tt * NE:(tt + 1) * NE]

            def vop(fn):
                em.op("vector", fn, sV)
                em.wait("vector", sV)
            vop(lambda e: e.reduce_max(col1[:, 0:1], psR[:, 0:NE], AX.X))
            vop(lambda e: e.tensor_scalar_mul(col1[:, 0:1], col1[:, 0:1], -1.0))
            em.wait("scalar", sV)
            em.op("scalar", lambda e: e.activation(
                rsc[:, 0:NE], psR[:, 0:NE], AF.Exp, bias=col1[:, 0:1]), sA)
            em.wait("vector", sA)
            vop(lambda e: e.reduce_sum(col1[:, 1:2], rsc[:, 0:NE], AX.X))
            vop(lambda e: e.reciprocal(col1[:, 2:3], col1[:, 1:2]))
            vop(lambda e, sc=sc: e.tensor_scalar_mul(sc, rsc[:, 0:NE], col1[:, 2:3]))
            if debug:
                em.op("vector", lambda e, tt=tt: e.tensor_copy(
                    scdbg[:, tt * NE:(tt + 1) * NE], sc), sV)
                em.wait("vector", sV)
            for g in range(4):
                em.op("vector", lambda e, sc=sc, g=g: e.reduce_max(
                    rsc[:, 16 + g:17 + g], sc[:, 4 * g:4 * g + 4], AX.X),
                    sV if g == 3 else None)
            em.wait("vector", sV)
            vop(lambda e: e.reduce_max(col1[:, 3:4], rsc[:, 16:20], AX.X))
            vop(lambda e: e.tensor_single_scalar(
                rsc[:, 20:24], rsc[:, 16:20], col1[:, 3:4], ALU.is_ge))
            vop(lambda e: e.scalar_tensor_tensor(
                rsc[:, 24:28], rsc[:, 20:24], -BIG, rsc[:, 16:20], ALU.mult, ALU.add))
            vop(lambda e: e.reduce_max(col1[:, 4:5], rsc[:, 24:28], AX.X))
            vop(lambda e: e.tensor_single_scalar(
                rsc[:, 28:32], rsc[:, 16:20], col1[:, 4:5], ALU.is_ge))
            for g in range(4):
                em.op("vector", lambda e, g=g: e.tensor_single_scalar(
                    gmask_sb[:, 4 * g:4 * g + 4], ones32[:, 0:4], rsc[:, 28 + g:29 + g],
                    ALU.mult), sV if g == 3 else None)
            em.wait("vector", sV)
            vop(lambda e: e.tensor_scalar(ms_sb[:, :], gmask_sb[:, :], BIG, -BIG,
                                          ALU.mult, ALU.add))
            vop(lambda e, sc=sc: e.tensor_add(ms_sb[:, :], ms_sb[:, :], sc))
            vop(lambda e: e.tensor_copy(rsc[:, 32:48], ms_sb[:, :]))
            for it in range(3):
                vop(lambda e: e.reduce_max(col1[:, 5:6], rsc[:, 32:48], AX.X))
                vop(lambda e: e.tensor_single_scalar(
                    rsc[:, 48:64], rsc[:, 32:48], col1[:, 5:6], ALU.is_ge))
                vop(lambda e: e.scalar_tensor_tensor(
                    rsc[:, 32:48], rsc[:, 48:64], -BIG, rsc[:, 32:48], ALU.mult, ALU.add))
            vop(lambda e: e.reduce_max(col1[:, 6:7], rsc[:, 32:48], AX.X))
            vop(lambda e: e.tensor_single_scalar(
                rsc[:, 48:64], ms_sb[:, :], col1[:, 6:7], ALU.is_ge))
            em.op("vector", lambda e, sc=sc: e.tensor_mul(sc, sc, rsc[:, 48:64]), sV)
            em.wait("tensor", sV)
            em.op("tensor", lambda e, sc=sc, tt=tt: e.transpose(
                psCT[0:16, tt * 128:(tt + 1) * 128], sc, id_sb[:, :]), sT)
        em.wait("vector", sT)
        em.op("vector", lambda e: e.tensor_copy(ct_sb[:, :], psCT[0:16, :]), sV)
        if debug:
            em.op("vector", lambda e: e.tensor_copy(ms_sb_dbg[:, :], ms_sb[:, :]), sV)
            em.op("vector", lambda e: e.tensor_copy(gmask_dbg[:, :], gmask_sb[:, :]), sV)
        for eng in ("gpsimd", "tensor", "scalar", "sync", "vector"):
            em.wait(eng, sV)
            em.wait(eng, sA)
            em.wait(eng, sT)
        rt_stack.close()
        w1 = sb("w1", [128, 4 * D], BF16)
        w2 = sb("w2", [128, 2 * SHINTER], BF16)
        hbuf = sb("hbuf2", [128, NIC1 * 1024], BF16)
        silu_sc = ystage[:, 0:1024]
        t1_sc = ystage[:, 1024:2048]
        inner_ps = ExitStack()
        psG = [inner_ps.enter_context(nc.psum_tensor(uname("psG0"), [128, 1024], F32)),
               inner_ps.enter_context(nc.psum_tensor(uname("psG1"), [128, 1024], F32))]
        psD = inner_ps.enter_context(nc.psum_tensor(uname("psDD"), [128, 1024], F32))
        psB2 = inner_ps.enter_context(nc.psum_tensor(uname("psB2"), [128, 1024], F32))
        state = {"w_used": [0, 0], "dw_used": [0, 0], "h_done": 0, "d_done": 0,
                 "d_sem": sA, "prev_h": 0}

        def gated_h(nic, wsrc1, wsrc3, cbc, tsl0, tw):
            em.wait("vector", sT)
            for ic in range(nic):
                slot = ic % 2
                em.wait("gpsimd", sT, state["w_used"][slot])
                em.dma("gpsimd",
                       w1[:, (2 * slot) * D:(2 * slot + 1) * D].rearrange("p (c f) -> p c f", c=NDC),
                       wsrc1[ic].rearrange("(c p) f -> p c f", p=128), sW)
                em.dma("gpsimd",
                       w1[:, (2 * slot + 1) * D:(2 * slot + 2) * D].rearrange("p (c f) -> p c f", c=NDC),
                       wsrc3[ic].rearrange("(c p) f -> p c f", p=128), sW)
                em.wait("tensor", sW)
                em.wait("tensor", sA)
                em.wait("tensor", sV)
                for dc in range(NDC):
                    for h0 in range(0, tw, 512):
                        em.op("tensor", lambda e, dc=dc, slot=slot, h0=h0, tsl0=tsl0: e.matmul(
                            psG[0][:, h0:h0 + 512],
                            w1[:, (2 * slot) * D + dc * 128:(2 * slot) * D + dc * 128 + 128],
                            xn[:, dc * 1024 + tsl0 + h0: dc * 1024 + tsl0 + h0 + 512],
                            start=(dc == 0), stop=(dc == NDC - 1)), None)
                for dc in range(NDC):
                    for h0 in range(0, tw, 512):
                        em.op("tensor", lambda e, dc=dc, slot=slot, h0=h0, tsl0=tsl0: e.matmul(
                            psG[1][:, h0:h0 + 512],
                            w1[:, (2 * slot + 1) * D + dc * 128:(2 * slot + 1) * D + dc * 128 + 128],
                            xn[:, dc * 1024 + tsl0 + h0: dc * 1024 + tsl0 + h0 + 512],
                            start=(dc == 0), stop=(dc == NDC - 1)),
                            sT if (dc == NDC - 1 and h0 + 512 >= tw) else None)
                state["w_used"][slot] = sT.n
                em.wait("scalar", sT, sT.n)
                em.wait("scalar", sV, state["prev_h"])
                em.op("scalar", lambda e, tw=tw: e.activation(
                    silu_sc[:, 0:tw], psG[0][:, 0:tw], AF.Sigmoid), sA)
                em.wait("vector", sA, sA.n)
                em.op("vector", lambda e, tw=tw: e.scalar_tensor_tensor(
                    t1_sc[:, 0:tw], psG[0][:, 0:tw], 1.0, silu_sc[:, 0:tw],
                    ALU.mult, ALU.mult))
                hb = hbuf[:, ic * tw:(ic + 1) * tw]
                if cbc is None:
                    em.op("vector", lambda e, hb=hb, tw=tw: e.tensor_mul(
                        hb, psG[1][:, 0:tw], t1_sc[:, 0:tw]), sV)
                else:
                    em.op("vector", lambda e, tw=tw: e.tensor_mul(
                        silu_sc[:, 0:tw], psG[1][:, 0:tw], t1_sc[:, 0:tw]))
                    em.op("vector", lambda e, hb=hb, cbc=cbc, tw=tw, tsl0=tsl0: e.tensor_mul(
                        hb, silu_sc[:, 0:tw], cbc[:, tsl0:tsl0 + tw]), sV)
                state["prev_h"] = sV.n
            state["h_done"] = sV.n

        def down_proj(nic, wsrc2, accumulate, tsl0, tw):
            for oc in range(NDC):
                slot = oc % 2
                em.wait("gpsimd", sT, state["dw_used"][slot])
                em.dma("gpsimd",
                       w2[:, slot * SHINTER: slot * SHINTER + nic * 128].rearrange("p (c f) -> p c f", c=nic),
                       wsrc2[oc].rearrange("(c p) f -> p c f", p=128), sW)
                em.wait("tensor", sW)
                em.wait("tensor", sV, state["h_done"])
                em.wait("tensor", state["d_sem"], state["d_done"])
                for ic in range(nic):
                    for h0 in range(0, tw, 512):
                        em.op("tensor", lambda e, ic=ic, slot=slot, h0=h0, nic=nic, tw=tw: e.matmul(
                            psD[:, h0:h0 + 512],
                            w2[:, slot * SHINTER + ic * 128: slot * SHINTER + ic * 128 + 128],
                            hbuf[:, ic * tw + h0: ic * tw + h0 + 512],
                            start=(ic == 0), stop=(ic == nic - 1)),
                            sT if (ic == nic - 1 and h0 + 512 >= tw) else None)
                state["dw_used"][slot] = sT.n
                if accumulate:
                    em.wait("vector", sT, sT.n)
                    em.op("vector", lambda e, oc=oc, tsl0=tsl0, tw=tw: e.tensor_add(
                        ybuf[:, oc * 1024 + tsl0: oc * 1024 + tsl0 + tw],
                        ybuf[:, oc * 1024 + tsl0: oc * 1024 + tsl0 + tw],
                        psD[:, 0:tw]), sV)
                    state["d_done"] = sV.n
                    state["d_sem"] = sV
                else:
                    em.wait("scalar", sT, sT.n)
                    em.op("scalar", lambda e, oc=oc, tsl0=tsl0, tw=tw: e.activation(
                        ybuf[:, oc * 1024 + tsl0: oc * 1024 + tsl0 + tw],
                        psD[:, 0:tw], AF.Copy), sA)
                    state["d_done"] = sA.n
                    state["d_sem"] = sA

        for th in range(2):
            gated_h(NICS, ext["l1_ws1"], ext["l1_ws3"], None, th * 512, 512)
            down_proj(NICS, ext["l1_ws2"], False, th * 512, 512)
        for e_idx in range(NE):
            em.wait("tensor", sV)
            for half in range(2):
                sl = slice(half * 512, half * 512 + 512)
                em.op("tensor", lambda e, e_idx=e_idx, sl=sl: e.matmul(
                    psB2[:, sl], oh_sb[:, e_idx * 128:(e_idx + 1) * 128], ct_sb[:, sl],
                    start=True, stop=True), sT if half == 1 else None)
            em.wait("vector", sT)
            gated_h(NIC1, ext["l1_w1"][e_idx], ext["l1_w3"][e_idx], psB2, 0, 1024)
            down_proj(NIC1, ext["l1_w2"][e_idx], True, 0, 1024)
        em.barrier(dma_sems=(sW, sX, sS))
        if debug:
            dy = dram("dbg_moe_ybuf", [128, NDC * 1024], F32, kind="ExternalOutput")
            em.dma("gpsimd", dy, ybuf[:, :], sW)
            dsc = dram("dbg_moe_scores", [128, 8 * 16], F32, kind="ExternalOutput")
            em.dma("gpsimd", dsc, scores_sb[:, :], sW)
            dct = dram("dbg_moe_ct", [16, 1024], F32, kind="ExternalOutput")
            em.dma("gpsimd", dct, ct_sb[:, :], sW)
            dms = dram("dbg_moe_ms", [128, 16], F32, kind="ExternalOutput")
            em.dma("gpsimd", dms, ms_sb_dbg[:, :], sW)
            dgm = dram("dbg_moe_gm", [128, 16], F32, kind="ExternalOutput")
            em.dma("gpsimd", dgm, gmask_dbg[:, :], sW)
            dsr = dram("dbg_moe_scraw", [128, 8 * 16], F32, kind="ExternalOutput")
            em.dma("gpsimd", dsr, scdbg[:, :], sW)
            em.barrier(dma_sems=(sW,))
        inner_ps.close()
        ln_stats_sbuf(lambda d: ybuf[:, d * 1024:(d + 1) * 1024])
        em.barrier()
        xst = ystage
        norm_done = []
        store_done = []
        for d in range(NDC):
            buf = xs[:, (d % 2) * 1024:(d % 2) * 1024 + 1024]
            ost = xst[:, (d % 2) * 1024:(d % 2) * 1024 + 1024]
            if d >= 2:
                em.wait("sync", sV, norm_done[d - 2])
                em.wait("vector", sS, store_done[d - 2])
            em.dma("sync", buf, xsrc[d * 128:(d + 1) * 128, :], sX)
            xn_d = sX.n
            em.op("vector", lambda e, d=d: e.tensor_mul(
                tsc[:, :], ybuf[:, d * 1024:(d + 1) * 1024], ab[:, 0:1024]))
            em.op("vector", lambda e: e.tensor_add(tsc[:, :], tsc[:, :], ab[:, 1024:2048]))
            em.wait("vector", sX, xn_d)
            em.op("vector", lambda e, ost=ost, buf=buf: e.tensor_add(ost, tsc[:, :], buf), sV)
            norm_done.append(sV.n)
            em.wait("scalar", sV, norm_done[d])
            em.dma("scalar", xdst[d * 128:(d + 1) * 128, :], ost, sS)
            store_done.append(sS.n)
        phase_close()

    src = x_first
    for i, s in enumerate(subs):
        dst = xout if i == len(subs) - 1 else xd[i % 2]
        if s == 0:
            attention(0, src, dst)
        elif s == 1:
            ffn0(src, dst)
        elif s == 2:
            attention(1, src, dst)
        else:
            moe(src, dst)
        src = dst

    if debug:
        em.barrier(dma_sems=(sW, sX, sS))
        o = dram("dbg_yd", [D, TL], F32, kind="ExternalOutput")
        em.dma("gpsimd", o, yd, sW)
    if debug and has_attn:
        em.barrier(dma_sems=(sW, sX, sS))
        dbg = {
            "dbg_xnl": (xnl, [D, TL], BF16),
            "dbg_xng": (xng, [NCORE * D, TL], BF16),
            "dbg_qd": (qd, [HPC * 128, T], BF16),
            "dbg_zl": (zl, [NCORE * HPC * 128, QB], BF16),
            "dbg_zato": (zato, [NCORE * HPC * 128, QB], BF16),
        }
        for name, (src, shape, dt_) in dbg.items():
            o = dram(name, shape, dt_, kind="ExternalOutput")
            em.dma("gpsimd", o, src, sW)

    em.wait("sync", sS)
    em.wait("sync", sW)
    em.wait("sync", sX)
    em.flush()
    st.close()
    return nc



def _percol(v, cols):
    return np.ascontiguousarray(np.asarray(v, np.float32).reshape(cols, 128).T)


def host_inputs(inp, core, subs=(0, 1, 2, 3), xin=None):
    c = core
    m = {}
    if subs[0] == 0:
        x = np.asarray(inp["L0_attn_x"])[0]
        m["x0"] = np.ascontiguousarray(x.T[:, c * TL:(c + 1) * TL], dtype=np.float32)
    else:
        m["xin"] = np.ascontiguousarray(xin[:, c * TL:(c + 1) * TL], dtype=np.float32)
    need = set(subs)
    hsl = slice(c * HPC * HD, (c + 1) * HPC * HD)

    def wT(w):
        return np.ascontiguousarray(np.asarray(w, np.float32).T)

    def tiles(wt, n_out):
        K = wt.shape[0]
        return np.ascontiguousarray(wt.reshape(K, n_out, 128).transpose(1, 0, 2))

    if 0 in need:
        m["l0_wq"] = np.ascontiguousarray(wT(inp["L0_attn_Wq"])[:, hsl])
        m["l0_wk"] = np.ascontiguousarray(wT(inp["L0_attn_Wk"])[:, hsl])
        m["l0_wv"] = np.ascontiguousarray(wT(inp["L0_attn_Wv"])[:, hsl])
        m["l0_wo"] = tiles(wT(inp["L0_attn_Wo"]), NDC)
        tri = np.zeros((128, 8, 1024), np.float32)
        kabs = np.arange(128)[:, None]
        qabs = np.arange(1024)[None, :]
        for j in range(8):
            tri[:, j, :] = np.where(128 * j + kabs <= qabs, 0.0, -BIG)
        m["c_tri"] = np.ascontiguousarray(tri.reshape(128, 8192))
    if 1 in need:
        m["l0_gw"] = tiles(wT(inp["L0_ffn_gate_w"]), NIC0)
        m["l0_uw"] = tiles(wT(inp["L0_ffn_up_w"]), NIC0)
        m["l0_dw"] = tiles(wT(inp["L0_ffn_down_w"]), NDC)
        m["l0_gs"] = _percol(inp["L0_ffn_gate_s"], NIC0)
        m["l0_us"] = _percol(inp["L0_ffn_up_s"], NIC0)
        m["l0_ds"] = _percol(inp["L0_ffn_down_s"], NDC)
    if 2 in need:
        m["l1_wq"] = np.ascontiguousarray(wT(inp["L1_attn_Wq"])[:, hsl])
        m["l1_wk"] = np.ascontiguousarray(wT(inp["L1_attn_Wk"])[:, hsl])
        m["l1_wv"] = np.ascontiguousarray(wT(inp["L1_attn_Wv"])[:, hsl])
        m["l1_wo"] = tiles(wT(inp["L1_attn_Wo"]), NDC)
        m["l1_bq"] = _percol(np.asarray(inp["L1_attn_bq"], np.float32)[hsl], HPC)
        m["l1_bk"] = _percol(np.asarray(inp["L1_attn_bk"], np.float32)[hsl], HPC)
        m["l1_bv"] = np.ascontiguousarray(
            np.asarray(inp["L1_attn_bv"], np.float32)[hsl][None, :])
        m["l1_bo"] = _percol(inp["L1_attn_bo"], NDC)
        g = np.zeros((1, 8), np.float32)
        for i, k in enumerate(["truth", "balance", "order", "justice", "harmony"]):
            g[0, i] = np.asarray(inp[f"L1_attn_{k}_gate"]).ravel()[0]
        m["l1_gates"] = g
    if 3 in need:
        m["l1_wg"] = wT(inp["L1_ffn_Wg"])
        w1 = np.asarray(inp["L1_ffn_W1"], np.float32)
        w3 = np.asarray(inp["L1_ffn_W3"], np.float32)
        w2 = np.asarray(inp["L1_ffn_W2"], np.float32)
        m["l1_w1"] = np.ascontiguousarray(
            w1.transpose(0, 2, 1).reshape(NE, D, NIC1, 128).transpose(0, 2, 1, 3))
        m["l1_w3"] = np.ascontiguousarray(
            w3.transpose(0, 2, 1).reshape(NE, D, NIC1, 128).transpose(0, 2, 1, 3))
        m["l1_w2"] = np.ascontiguousarray(
            w2.transpose(0, 2, 1).reshape(NE, INTER1, NDC, 128).transpose(0, 2, 1, 3))
        m["l1_ws1"] = tiles(wT(inp["L1_ffn_Ws1"]), NICS)
        m["l1_ws3"] = tiles(wT(inp["L1_ffn_Ws3"]), NICS)
        m["l1_ws2"] = tiles(wT(inp["L1_ffn_Ws2"]), NDC)
        oh = np.zeros((NE, NE * 128), np.float32)
        for e_ in range(NE):
            oh[e_, e_ * 128:(e_ + 1) * 128] = 1.0
        m["c_onehot"] = oh
        m["c_id"] = np.eye(128, dtype=np.float32)
    m["c_ones"] = np.ones((128, 128), np.float32)
    if 0 in need or 2 in need:
        pos = (np.arange(NKT)[None, :] * 128 + np.arange(128)[:, None]).astype(np.float32) / T
        m["c_pos"] = np.ascontiguousarray(pos)
    return m


def run_subs(inp, subs=(0, 1, 2, 3), xin=None, trace=False, debug=False):
    nc = build_program(subs, debug=debug)
    in_maps = [host_inputs(inp, c, subs, xin) for c in range(NCORE)]
    res = run_bass_kernel_spmd(nc, in_maps, core_ids=list(range(NCORE)), trace=trace)
    out = np.concatenate([res.results[c]["xout"] for c in range(NCORE)], axis=1)
    return out, res



def kernel(**inputs):
    x = None
    for s in range(4):
        subs = (s,)
        nc = build_program(subs)
        in_maps = [host_inputs(inputs, c, subs, xin=x) for c in range(NCORE)]
        res = run_bass_kernel_spmd(nc, in_maps, core_ids=list(range(NCORE)))
        x = np.concatenate([res.results[c]["xout"] for c in range(NCORE)], axis=1)
    return np.ascontiguousarray(x.T).reshape(1, T, D).astype(np.float32)
